# version 13 (final)
# baseline (speedup 1.0000x reference)
import contextlib

import numpy as np

import concourse.bass as bass
import concourse.mybir as mybir
from concourse.bass_utils import run_bass_kernel_spmd

F32 = mybir.dt.float32
BF16 = mybir.dt.bfloat16
AF = mybir.ActivationFunctionType
ALU = mybir.AluOpType
AX = mybir.AxisListType

T = 4096
D = 2048
KD = D // 128
SEG = 512
NSEG = T // SEG
NH = 16
HID = 8192
FH = 4096
KH = HID // 128
KF = FH // 128
L = 8
EPS = 1e-5


class Sem:
    def __init__(self, nc, es, name):
        self.h = es.enter_context(nc.semaphore(name))
        self.n = 0

    def inc(self, inst, k=1):
        self.n += k
        inst.then_inc(self.h, k)
        return self.n


def _w(eng, sem, val):
    if val > 0:
        eng.wait_ge(sem.h, val)


def _tw(eng, tok):
    if tok is None:
        return
    if isinstance(tok, list):
        for t in tok:
            _tw(eng, t)
        return
    s, v = tok
    if s is not None and v:
        eng.wait_ge(s.h, v)


def make_identity(nc, identity):
    sq = identity.shape[0]
    nc.gpsimd.memset(identity, 0.0)
    nc.gpsimd.affine_select(
        out=identity, in_=identity, compare_op=ALU.not_equal,
        fill=1.0, base=0, pattern=[[-1, sq]], channel_multiplier=1,
    )


def build(phases=("p1", "ssm", "p2"), debug=()):
    nc = bass.Bass("TRN2", target_bir_lowering=False)
    dbg = set(debug)
    kind = lambda n: "ExternalOutput" if n in dbg else "Internal"

    x = nc.dram_tensor("x", [T, D], F32, kind="ExternalInput").ap()
    inw = nc.dram_tensor("inw", [D, D], F32, kind="ExternalInput").ap()
    outw = nc.dram_tensor("outw", [D, D], F32, kind="ExternalInput").ap()
    w1 = nc.dram_tensor("w1", [D, HID], F32, kind="ExternalInput").ap()
    w2 = nc.dram_tensor("w2", [HID, D], F32, kind="ExternalInput").ap()
    W1T = nc.dram_tensor("W1T", [D, FH], F32, kind="ExternalInput").ap()
    W2T = nc.dram_tensor("W2T", [FH, FH], F32, kind="ExternalInput").ap()
    W3T = nc.dram_tensor("W3T", [FH, D], F32, kind="ExternalInput").ap()
    sA = nc.dram_tensor("sA", [NH, 128, 128], F32, kind="ExternalInput").ap()
    sB = nc.dram_tensor("sB", [NH, 128, 128], F32, kind="ExternalInput").ap()
    sAT = nc.dram_tensor("sAT", [NH, 128, 128], F32, kind="ExternalInput").ap()
    sBT = nc.dram_tensor("sBT", [NH, 128, 128], F32, kind="ExternalInput").ap()
    sCT = nc.dram_tensor("sCT", [NH, 128, 128], F32, kind="ExternalInput").ap()
    sDT = nc.dram_tensor("sDT", [NH, 128, 128], F32, kind="ExternalInput").ap()
    vec_specs = {
        "ln1w_c": KD, "ln1b_c": KD, "inb_c": KD, "outb_c": KD,
        "b1_c": KH, "ln2w_c": KD, "ln2b_c": KD, "fb1_c": KF, "fb2_c": KF,
    }
    vin = {k: nc.dram_tensor(k, [128, n], F32, kind="ExternalInput").ap()
           for k, n in vec_specs.items()}
    b2_row = nc.dram_tensor("b2_row", [1, D], F32, kind="ExternalInput").ap()
    fb3_row = nc.dram_tensor("fb3_row", [1, D], F32, kind="ExternalInput").ap()
    out = nc.dram_tensor("out", [T, D], F32, kind="ExternalOutput").ap()

    outw16 = nc.dram_tensor("outw16", [D, D], BF16, kind=kind("outw16")).ap()
    w116 = nc.dram_tensor("w116", [D, HID], BF16, kind="Internal").ap()
    w216 = nc.dram_tensor("w216", [HID, D], BF16, kind="Internal").ap()
    W1T16 = nc.dram_tensor("W1T16", [D, FH], BF16, kind="Internal").ap()
    W2T16 = nc.dram_tensor("W2T16", [FH, FH], BF16, kind="Internal").ap()
    W3T16 = nc.dram_tensor("W3T16", [FH, D], BF16, kind="Internal").ap()
    x16 = nc.dram_tensor("x16", [T, D], BF16, kind="Internal").ap()
    u16 = nc.dram_tensor("u16", [D, T], BF16, kind=kind("u16")).ap()
    y16 = nc.dram_tensor("y16", [D, T], BF16, kind=kind("y16")).ap()
    xnf16 = nc.dram_tensor("xnf16", [D, T], BF16, kind=kind("xnf16")).ap()
    h1dbg = (nc.dram_tensor("h1dbg", [D, T], BF16, kind="ExternalOutput").ap()
             if "h1dbg" in dbg else None)

    with contextlib.ExitStack() as es:
        sb = lambda name, shape, dt: es.enter_context(
            nc.sbuf_tensor(name, shape, dt))
        arena1 = sb("arena1", [128, 32768], BF16)
        arena2 = sb("arena2", [128, 16384], BF16)
        arena3 = sb("arena3", [128, 16384], BF16)
        arena4 = sb("arena4", [128, 8192], BF16)
        wslab = sb("wslab", [128, 8192], BF16)
        facc = sb("facc", [128, 8192], F32)
        ident = sb("ident", [128, 128], BF16)
        vecs = {k: sb(k + "_s", [128, n], F32) for k, n in vec_specs.items()}
        onescol = sb("onescol", [1, 128], BF16)
        b23 = sb("b23", [1, D], BF16)
        epsc = sb("epsc", [128, 1], F32)
        zcol = sb("zcol", [128, 2], BF16)
        st6 = sb("st6", [128, 24], F32)
        st8 = sb("st8", [128, 32], F32)
        stc = sb("stc", [128, 32], F32)
        stm = sb("stm", [128, 64], F32)

        psA = es.enter_context(nc.psum_tensor("psA", [128, 2048], F32))
        psB = es.enter_context(nc.psum_tensor("psB", [128, 1024], F32))
        psT = es.enter_context(nc.psum_tensor("psT", [128, 1024], BF16))
        psT2 = es.enter_context(nc.psum_tensor("psT2", [128, 1024], BF16))
        psTs = [psT, psT2]

        dW = Sem(nc, es, "dW")
        dS = Sem(nc, es, "dS")
        dG = Sem(nc, es, "dG")
        cT = Sem(nc, es, "cT")
        cV = Sem(nc, es, "cV")
        cA = Sem(nc, es, "cA")
        cG = Sem(nc, es, "cG")

        block = es.enter_context(nc.Block())

        b23f = facc[0:1, 0:2 * D]

        pre_tok = {}

        def p0_gp(g):
            g.memset(ident[:], 0.0).then_inc(cG.h, 1)
            g.wait_ge(cG.h, cG.n + 1)
            g.affine_select(out=ident[:], in_=ident[:],
                            compare_op=ALU.not_equal, fill=1.0, base=0,
                            pattern=[[-1, 128]],
                            channel_multiplier=1).then_inc(cG.h, 1)
            g.memset(epsc[:], EPS).then_inc(cG.h, 1)
            g.memset(zcol[:], 0.0).then_inc(cG.h, 1)
            g.memset(onescol[:], 1.0).then_inc(cG.h, 1)
            cG.n += 5
            for k in range(KD):
                g.dma_start(arena1[:, k * 2048:(k + 1) * 2048],
                            inw[k * 128:(k + 1) * 128, :],
                            max_dma_last_dim=8192).then_inc(dG.h, 16)
                dG.n += 16
            pre_tok["inw"] = dG.n
            for r0 in range(0, T, 512):
                g.dma_start(x16[r0:r0 + 512, :], x[r0:r0 + 512, :],
                            max_dma_last_dim=8192).then_inc(dG.h, 16)
                dG.n += 16
                pre_tok["x16_%d" % (r0 // 512)] = dG.n
            g.dma_start(out[:, :], x[:, :]).then_inc(dG.h, 16)
            dG.n += 16
            pre_tok["outcopy"] = dG.n
            for srcw, dstw, rows, nm in (
                    (outw, outw16, D, "outw16"), (W1T, W1T16, D, "W1T16"),
                    (W2T, W2T16, FH, "W2T16"), (w1, w116, D, "w116"),
                    (w2, w216, HID, "w216"), (W3T, W3T16, FH, "W3T16")):
                for r0 in range(0, rows, 2048):
                    g.dma_start(dstw[r0:r0 + 2048, :], srcw[r0:r0 + 2048, :],
                                max_dma_last_dim=8192).then_inc(dG.h, 16)
                    dG.n += 16
                pre_tok[nm] = dG.n

        block.gpsimd(p0_gp)
        ident_tok = cG.n
        inw16s = arena1[:, 0:KD * 2048]

        def p0_sync(sy):
            for k, ap_ in vin.items():
                sy.dma_start(vecs[k][:], ap_[:]).then_inc(dW.h, 16)
                dW.n += 16
            sy.dma_start(b23f[:, 0:D], b2_row[:]).then_inc(dW.h, 16)
            dW.n += 16
            sy.dma_start(b23f[:, D:2 * D], fb3_row[:]).then_inc(dW.h, 16)
            dW.n += 16

        block.sync(p0_sync)
        vecs_tok = dW.n

        def p0_vec(v):
            _w(v, dW, vecs_tok)
            ins = v.tensor_add(b23f[:, 0:D], b23f[:, 0:D], b23f[:, D:2 * D])
            cV.inc(ins)
            v.wait_ge(cV.h, cV.n)
            cV.inc(v.tensor_copy(b23[:], b23f[:, 0:D]))

        block.vector(p0_vec)
        b23_tok = cV.n

        block.scalar(lambda s: (s.wait_ge(dW.h, vecs_tok),
                                s.wait_ge(cG.h, ident_tok)))

        def stats_batch(tiles, gates=None, skip_s2=False, base=0):
            nt = len(tiles)
            sm = stm[:, base * 32:base * 32 + 32]

            def dv1(v):
                for t, tile in enumerate(tiles):
                    if gates is not None:
                        _tw(v, gates[t])
                    ins = None
                    for k in range(4):
                        ins = v.bn_stats(st6[:, 6 * k:6 * k + 6],
                                         tile[:, 512 * k:512 * (k + 1)])
                    cV.inc(ins)
                    v.wait_ge(cV.h, cV.n)
                    for k in range(4):
                        ins = v.bn_aggr(st8[:, 8 * t + 2 * k:8 * t + 2 * k + 2],
                                        st6[:, 6 * k:6 * k + 6])
                    cV.inc(ins)
                v.wait_ge(cV.h, cV.n)
                nc_ = 4 * nt
                means = st8[:, 0:8 * nt:2]
                vars_ = st8[:, 1:8 * nt:2]
                cV.inc(v.tensor_tensor(stc[:, 0:nc_], means, means,
                                       op=ALU.mult))
                v.wait_ge(cV.h, cV.n)
                cV.inc(v.tensor_add(stc[:, 0:nc_], stc[:, 0:nc_], vars_))
                v.wait_ge(cV.h, cV.n)
                for srcv, dcol in ((means, 0), (stc[:, 0:nc_], 1)):
                    cV.inc(v.tensor_add(stc[:, 16:16 + nt],
                                        srcv[:, 0:nc_:4], srcv[:, 1:nc_:4]))
                    v.wait_ge(cV.h, cV.n)
                    cV.inc(v.tensor_add(stc[:, 24:24 + nt],
                                        srcv[:, 2:nc_:4], srcv[:, 3:nc_:4]))
                    v.wait_ge(cV.h, cV.n)
                    cV.inc(v.tensor_add(stc[:, 16:16 + nt], stc[:, 16:16 + nt],
                                        stc[:, 24:24 + nt]))
                    v.wait_ge(cV.h, cV.n)
                    cV.inc(v.tensor_scalar_mul(
                        sm[:, dcol:8 * nt:8], stc[:, 16:16 + nt], 0.25))
                    v.wait_ge(cV.h, cV.n)
                cV.inc(v.tensor_tensor(stc[:, 16:16 + nt], sm[:, 0:8 * nt:8],
                                       sm[:, 0:8 * nt:8], op=ALU.mult))
                v.wait_ge(cV.h, cV.n)
                cV.inc(v.tensor_sub(sm[:, 1:8 * nt:8], sm[:, 1:8 * nt:8],
                                    stc[:, 16:16 + nt]))

            block.vector(dv1)
            var_tok = cV.n

            def ac1(s):
                _w(s, cV, var_tok)
                cA.inc(s.activation(sm[:, 3:8 * nt:8], sm[:, 1:8 * nt:8],
                                    AF.Sqrt, bias=epsc[:], scale=1.0))

            block.scalar(ac1)
            sq1_tok = cA.n

            def dv2(v):
                _w(v, cA, sq1_tok)
                cV.inc(v.reciprocal(sm[:, 2:8 * nt:8], sm[:, 3:8 * nt:8]))
                if skip_s2:
                    v.wait_ge(cV.h, cV.n)
                    return
                v.wait_ge(cV.h, cV.n)
                cV.inc(v.tensor_tensor(sm[:, 4:8 * nt:8], sm[:, 1:8 * nt:8],
                                       sm[:, 2:8 * nt:8], op=ALU.mult))
                v.wait_ge(cV.h, cV.n)
                cV.inc(v.tensor_tensor(sm[:, 4:8 * nt:8], sm[:, 4:8 * nt:8],
                                       sm[:, 2:8 * nt:8], op=ALU.mult))

            block.vector(dv2)
            if skip_s2:
                return cV.n
            t2_tok = cV.n

            def ac2(s):
                _w(s, cV, t2_tok)
                cA.inc(s.activation(sm[:, 5:8 * nt:8], sm[:, 4:8 * nt:8],
                                    AF.Sqrt, bias=epsc[:], scale=1.0))

            block.scalar(ac2)
            sq2_tok = cA.n

            def dv3(v):
                _w(v, cA, sq2_tok)
                cV.inc(v.reciprocal(sm[:, 6:8 * nt:8], sm[:, 5:8 * nt:8]))
                v.wait_ge(cV.h, cV.n)
                cV.inc(v.tensor_tensor(sm[:, 7:8 * nt:8], sm[:, 2:8 * nt:8],
                                       sm[:, 6:8 * nt:8], op=ALU.mult))
                v.wait_ge(cV.h, cV.n)

            block.vector(dv3)
            return cV.n

        h1seg = arena3[:, 0:8192]
        xnfseg = arena3[:, 8192:16384]
        xntoks = [arena4[:, 0:2048], arena4[:, 2048:4096]]
        xhtoks = [arena4[:, 4096:6144], arena4[:, 6144:8192]]
        ust = wslab[:, 0:8192]
        xtile = [arena2[:, i * 2048:(i + 1) * 2048] for i in range(8)]

        xt_free = [None] * 8
        tok_free = [None, None]
        psT_last = [{"cV": 0, "cA": 0}, {"cV": 0, "cA": 0}]
        ip_ps = [psA[:, i * 512:(i + 1) * 512] for i in range(4)] + \
                [psB[:, i * 512:(i + 1) * 512] for i in range(2)]
        ip_free = [None] * 6
        psB_free = [None, None]
        ust_free = [None, None]
        xn_store_prev = 0
        h1_store_prev = 0
        loop1_uev_last = 0
        stats_toks = {}

        def p1_load_seg(seg):
            toks = []
            for tt in range(4):
                sl = (seg % 2) * 4 + tt
                r0 = seg * SEG + tt * 128

                def ld(sy):
                    _tw(sy, xt_free[sl])
                    if tt == 0:
                        _w(sy, dG, pre_tok["x16_%d" % seg])
                    sy.dma_start(xtile[sl][:], x16[r0:r0 + 128, :]
                                 ).then_inc(dW.h, 16)
                    dW.n += 16

                block.sync(ld)
                toks.append((dW, dW.n))
            return toks

        if "p1" in phases:
            toks0 = p1_load_seg(0)
            stats_batch([xtile[tt] for tt in range(4)], gates=toks0, base=0)
            stats_toks[0] = cV.n
            for seg in range(NSEG):
                if seg + 1 < NSEG:
                    nt = p1_load_seg(seg + 1)
                    stats_batch([xtile[4 * ((seg + 1) % 2) + tt]
                                 for tt in range(4)], gates=nt,
                                base=(seg + 1) % 2)
                    stats_toks[seg + 1] = cV.n

                sb_ = (seg % 2) * 32
                for tt in range(4):
                    sl = tt % 2
                    xsl = (seg % 2) * 4 + tt
                    xntok = xntoks[sl]
                    xhtok = xhtoks[sl]

                    def p1_apply(v):
                        if tt == 0:
                            _w(v, cV, stats_toks[seg])
                        _tw(v, tok_free[sl])
                        v.tensor_scalar(xntok[:], xtile[xsl][:],
                                        stm[:, sb_ + 8 * tt:sb_ + 8 * tt + 1],
                                        stm[:, sb_ + 8 * tt + 2:
                                               sb_ + 8 * tt + 3],
                                        op0=ALU.subtract, op1=ALU.mult)
                        cV.inc(v.tensor_scalar(
                            xhtok[:], xtile[xsl][:],
                            stm[:, sb_ + 8 * tt:sb_ + 8 * tt + 1],
                            stm[:, sb_ + 8 * tt + 7:sb_ + 8 * tt + 8],
                            op0=ALU.subtract, op1=ALU.mult))

                    block.vector(p1_apply)
                    apply_tok = cV.n
                    xt_free[xsl] = (cV, apply_tok)

                    for pair in range(4):
                        src = xntok if pair < 2 else xhtok
                        pslot = pair % 2
                        ps = psTs[pslot]
                        c0 = 8 * (pair % 2)

                        def p1_tr(t):
                            if seg == 0 and tt == 0 and pair == 0:
                                _w(t, cG, ident_tok)
                            if pair == 0:
                                _w(t, cV, apply_tok)
                            _w(t, cV, psT_last[pslot]["cV"])
                            _w(t, cA, psT_last[pslot]["cA"])
                            ins = None
                            for j in range(8):
                                c = c0 + j
                                ins = t.transpose(
                                    ps[:, j * 128:(j + 1) * 128],
                                    src[:, c * 128:(c + 1) * 128], ident[:])
                            cT.inc(ins)

                        block.tensor(p1_tr)
                        tr_tok = cT.n

                        if pair < 2:
                            view = xnfseg.rearrange("p (c q j) -> p c q j",
                                                    c=16, q=4, j=128)

                            def p1_evxn(v):
                                _w(v, cT, tr_tok)
                                if seg > 0 and tt == 0 and pair == 0:
                                    _w(v, dS, xn_store_prev)
                                pv = ps.rearrange("p (c j) -> p c j", j=128)
                                cV.inc(v.tensor_copy(
                                    view[:, c0:c0 + 8, tt, :], pv))

                            block.vector(p1_evxn)
                            psT_last[pslot]["cV"] = cV.n
                        else:
                            def p1_evh1(v):
                                _w(v, cT, tr_tok)
                                if tt == 0 and pair == 2:
                                    _w(v, dS, h1_store_prev)
                                ins = None
                                for j in range(8):
                                    c = c0 + j
                                    ins = v.tensor_scalar(
                                        h1seg[:, c * 512 + tt * 128:
                                                 c * 512 + tt * 128 + 128],
                                        ps[:, j * 128:(j + 1) * 128],
                                        vecs["ln1w_c"][:, c:c + 1],
                                        vecs["ln1b_c"][:, c:c + 1],
                                        op0=ALU.mult, op1=ALU.add)
                                cV.inc(ins)

                            block.vector(p1_evh1)
                            psT_last[pslot]["cV"] = cV.n
                    tok_free[sl] = (cT, cT.n)
                seg_xn_done = cV.n
                seg_h1_done = cV.n

                def p1_stxn(s):
                    _w(s, cV, seg_xn_done)
                    cA.inc(s.copy(xnfseg[:], xnfseg[:]))
                    s.wait_ge(cA.h, cA.n)
                    s.dma_start(
                        xnf16.rearrange("(c p) t -> p c t", p=128)[
                            :, :, seg * SEG:(seg + 1) * SEG],
                        xnfseg.rearrange("p (c t) -> p c t", c=KD)
                    ).then_inc(dS.h, 16)
                    dS.n += 16

                block.scalar(p1_stxn)
                xn_store_prev = dS.n

                if h1dbg is not None:
                    def p1_sth1(s):
                        _w(s, cV, seg_h1_done)
                        for c in range(KD):
                            s.dma_start(
                                h1dbg[c * 128:(c + 1) * 128,
                                      seg * SEG:(seg + 1) * SEG],
                                h1seg[:, c * 512:(c + 1) * 512]
                            ).then_inc(dS.h, 16)
                            dS.n += 16

                    block.scalar(p1_sth1)
                    h1_store_prev = dS.n

                for fo in range(KD):
                    slot = fo % 6
                    ps = ip_ps[slot]

                    def p1_ip(t):
                        if fo == 0:
                            _w(t, cV, seg_h1_done)
                            if seg == 0:
                                _w(t, dG, pre_tok["inw"])
                        _tw(t, ip_free[slot])
                        ins = None
                        for k in range(KD):
                            ins = t.matmul(
                                ps, inw16s[:, k * 2048 + fo * 128:
                                              k * 2048 + (fo + 1) * 128],
                                h1seg[:, k * 512:(k + 1) * 512],
                                start=(k == 0), stop=(k == KD - 1))
                        cT.inc(ins)

                    block.tensor(p1_ip)
                    mm_tok = cT.n

                    def p1_uev(v):
                        _w(v, cT, mm_tok)
                        if fo == 0:
                            _tw(v, ust_free[0])
                        cV.inc(v.tensor_scalar(
                            ust[:, fo * 512:(fo + 1) * 512], ps,
                            vecs["inb_c"][:, fo:fo + 1], None, op0=ALU.add))

                    block.vector(p1_uev)
                    ip_free[slot] = (cV, cV.n)
                seg_uev_done = cV.n

                def p1_ust(s):
                    _w(s, cV, seg_uev_done)
                    cA.inc(s.copy(ust[:], ust[:]))
                    s.wait_ge(cA.h, cA.n)
                    s.dma_start(
                        u16.rearrange("(c p) t -> p c t", p=128)[
                            :, :, seg * SEG:(seg + 1) * SEG],
                        ust.rearrange("p (c t) -> p c t", c=KD)
                    ).then_inc(dS.h, 16)
                    dS.n += 16

                block.scalar(p1_ust)
                ust_free[0] = (dS, dS.n)
                loop1_uev_last = cV.n

        loop1_done_cT = cT.n
        loop1_u_done = dS.n

        mats = {}
        for i, nm in enumerate(("AT", "BT", "CT", "DT", "A", "B", "CA", "CBD")):
            mats[nm] = arena1[:, i * 2048:(i + 1) * 2048]
        NP = 3
        CHB = 16384
        off = CHB
        chainP, hrotP, wlowP, ebufP = [], [], [], []
        for p in range(NP):
            chainP.append(arena1[:, off:off + 1664]); off += 1664
        for p in range(NP):
            hrotP.append(arena1[:, off:off + 512]); off += 512
        for p in range(NP):
            wlowP.append(arena1[:, off:off + 512]); off += 512
        for p in range(NP):
            ebufP.append(arena1[:, off:off + 512]); off += 512
        wfpP = [facc[:, 1024 * p:1024 * (p + 1)] for p in range(NP)]

        def p_mats(g):
            g.wait_ge(cT.h, loop1_done_cT)
            for src, nm in ((sAT, "AT"), (sBT, "BT"), (sCT, "CT"),
                            (sDT, "DT"), (sA, "A"), (sB, "B")):
                g.dma_start(mats[nm].rearrange("p (h j) -> p h j", h=NH),
                            src.rearrange("h i j -> i h j"),
                            max_dma_last_dim=512).then_inc(dG.h, 16)
                dG.n += 16

        block.gpsimd(p_mats)
        mats_tok = dG.n

        uh = [arena2[:, 0:4096], arena2[:, 4096:8192],
              wslab[:, 0:4096]]
        upk = [arena2[:, 8192:12288], arena2[:, 12288:16384],
               wslab[:, 4096:8192]]
        ystag = [arena4[:, 0:4096], arena4[:, 4096:8192],
                 arena3[:, 0:4096]]

        uh_free = [None, None, None]
        upk_free = [(cV, loop1_uev_last)] * 3
        ystag_free = [None, None, None]
        psSt = [psA[:, 0:512], psA[:, 512:1024], psA[:, 1024:1536]]
        psAux = [psA[:, 1536:2048], psB[:, 0:512], psB[:, 512:1024]]
        psSt_free = [(cV, loop1_uev_last)] * 3
        psAux_free = [(cV, loop1_uev_last)] * 3
        ssm_end_cT = loop1_done_cT
        ssm_y_done = loop1_u_done

        def ssm_head_thunks(h):
            hs = h % NP
            mat = lambda nm: mats[nm][:, h * 128:(h + 1) * 128]
            chain = chainP[hs]
            hrot = hrotP[hs]
            wlow = wlowP[hs]
            ebuf = ebufP[hs]
            wfp = wfpP[hs]
            pst = psSt[hs]
            pax = psAux[hs]
            st = {"uh_tok": 0, "upk_tok": 0, "ca_mm": 0, "ca_done": 0,
                  "hev": {}, "w_tok": 0, "e_tok": 0, "chain_m": {},
                  "mslot": {}, "wsrc": 0, "wdst": 512, "yev": 0}
            thunks = []

            def t_load():
                def f(sy):
                    _tw(sy, uh_free[hs])
                    _w(sy, dS, loop1_u_done)
                    sy.dma_start(uh[hs][:], u16[h * 128:(h + 1) * 128, :]
                                 ).then_inc(dW.h, 16)
                    dW.n += 16
                block.sync(f)
                st["uh_tok"] = dW.n
            thunks.append(t_load)

            def t_repack():
                def f(g):
                    _w(g, dW, st["uh_tok"])
                    _tw(g, upk_free[hs])
                    for s in range(L):
                        cG.inc(g.tensor_copy(
                            upk[hs][:, s * 512:(s + 1) * 512],
                            uh[hs][:, s:4096:8]))
                block.gpsimd(f)
                st["upk_tok"] = cG.n
                uh_free[hs] = (cG, cG.n)
            thunks.append(t_repack)

            def t_ca():
                def f(t):
                    if h == 0:
                        _w(t, dG, mats_tok)
                    _tw(t, psAux_free[hs])
                    t.matmul(pax[:, 0:128], mat("A"), mat("CT"),
                             start=True, stop=True)
                    cT.inc(t.matmul(pax[:, 128:256], mat("B"), mat("CT"),
                                    start=True, stop=True))
                block.tensor(f)
                st["ca_mm"] = cT.n

                def f2(v):
                    _w(v, cT, st["ca_mm"])
                    v.tensor_copy(mat("CA"), pax[:, 0:128])
                    cV.inc(v.tensor_tensor(mat("CBD"), pax[:, 128:256],
                                           mat("DT"), op=ALU.add))
                block.vector(f2)
                psAux_free[hs] = (cV, cV.n)
                st["ca_done"] = cV.n
            thunks.append(t_ca)

            def chain_level(lvl):
                if lvl == 1:
                    lhsP, rhsP = mat("AT"), mat("A")
                    lhsM, rhsM = mat("A"), mat("AT")
                    prev_t = st["ca_done"]
                else:
                    pm = st["mslot"][lvl - 1]
                    pp = (lvl - 1) % 2
                    lhsP = chain[:, pm * 128:(pm + 1) * 128]
                    rhsP = chain[:, pp * 128:(pp + 1) * 128]
                    lhsM = chain[:, pp * 128:(pp + 1) * 128]
                    rhsM = chain[:, pm * 128:(pm + 1) * 128]
                    prev_t = st["chain_m"][lvl - 1]
                nxt_p = lvl % 2
                nxt_m = 2 + (lvl % 2) if lvl <= 2 else 4 + (lvl - 3)
                st["mslot"][lvl] = nxt_m

                def f(t):
                    _w(t, cV, prev_t)
                    _tw(t, psAux_free[hs])
                    t.matmul(pax[:, 0:128], lhsP, rhsP, start=True, stop=True)
                    cT.inc(t.matmul(pax[:, 128:256], lhsM, rhsM,
                                    start=True, stop=True))
                block.tensor(f)
                mm_tok = cT.n

                def f2(v):
                    _w(v, cT, mm_tok)
                    v.tensor_copy(chain[:, nxt_p * 128:(nxt_p + 1) * 128],
                                  pax[:, 0:128])
                    cV.inc(v.tensor_copy(
                        chain[:, nxt_m * 128:(nxt_m + 1) * 128],
                        pax[:, 128:256]))
                block.vector(f2)
                st["chain_m"][lvl] = cV.n
                psAux_free[hs] = (cV, cV.n)

            def t_p1(s):
                def th():
                    def f(t):
                        if s == 0:
                            _w(t, cG, st["upk_tok"])
                            _tw(t, psSt_free[hs])
                        else:
                            _w(t, cV, st["hev"][s - 1])
                        if s == 0:
                            ins = t.matmul(pst, mat("BT"), upk[hs][:, 0:512],
                                           start=True, stop=True)
                        else:
                            t.matmul(pst, mat("BT"),
                                     upk[hs][:, s * 512:(s + 1) * 512],
                                     start=True, stop=False)
                            ins = t.matmul(pst, mat("AT"), hrot[:, 0:512],
                                           start=False, stop=True)
                        cT.inc(ins)
                    block.tensor(f)
                    mm_tok = cT.n

                    def f2(v):
                        _w(v, cT, mm_tok)
                        if s < L - 1:
                            cV.inc(v.tensor_copy(hrot[:, 0:512], pst))
                            st["hev"][s] = cV.n
                        else:
                            cV.inc(v.tensor_copy(wfp[:, 0:512], pst))
                            st["w_tok"] = cV.n
                    block.vector(f2)
                    psSt_free[hs] = (cV, cV.n)
                    chain_level(s + 1)
                return th
            for s in range(L):
                thunks.append(t_p1(s))

            def t_chain_rest():
                for lvl in range(9, 12):
                    chain_level(lvl)
            thunks.append(t_chain_rest)

            def t_hil(d):
                def th():
                    k = 1 << (d - 1)
                    mm_sl = st["mslot"][d + 2]
                    mlh = chain[:, mm_sl * 128:(mm_sl + 1) * 128]
                    m_tok = st["chain_m"][d + 2]
                    wsrc, wdst = st["wsrc"], st["wdst"]

                    def f(v):
                        _w(v, cV, st["w_tok"])
                        cV.inc(v.tensor_copy(wlow[:, 0:512 - k],
                                             wfp[:, wsrc:wsrc + 512 - k]))
                        v.wait_ge(cV.h, cV.n)
                    block.vector(f)
                    wl_tok = cV.n

                    def f2(t):
                        _w(t, cV, max(m_tok, wl_tok))
                        _tw(t, psAux_free[hs])
                        cT.inc(t.matmul(pax[:, 0:512 - k], mlh,
                                        wlow[:, 0:512 - k],
                                        start=True, stop=True))
                    block.tensor(f2)
                    hil_tok = cT.n

                    def f3(v):
                        _w(v, cT, hil_tok)
                        v.tensor_copy(wfp[:, wdst:wdst + k],
                                      wfp[:, wsrc:wsrc + k])
                        cV.inc(v.tensor_add(wfp[:, wdst + k:wdst + 512],
                                            wfp[:, wsrc + k:wsrc + 512],
                                            pax[:, 0:512 - k]))
                    block.vector(f3)
                    st["w_tok"] = cV.n
                    psAux_free[hs] = (cV, cV.n)
                    st["wsrc"], st["wdst"] = wdst, wsrc
                return th
            for d in range(1, 10):
                thunks.append(t_hil(d))

            def t_e():
                def f(v):
                    _w(v, cV, st["w_tok"])
                    v.tensor_copy(ebuf[:, 0:1], zcol[:, 0:1])
                    cV.inc(v.tensor_copy(
                        ebuf[:, 1:512], wfp[:, st["wsrc"]:st["wsrc"] + 511]))
                    v.wait_ge(cV.h, cV.n)
                block.vector(f)
                st["e_tok"] = cV.n
            thunks.append(t_e)

            def t_p2(s):
                def th():
                    def f(t):
                        if s == 0:
                            _w(t, cV, st["e_tok"])
                            _tw(t, psSt_free[hs])
                            hprev = ebuf[:, 0:512]
                        else:
                            _w(t, cV, st["hev"][100 + s - 1])
                            hprev = hrot[:, 0:512]
                        u_s = upk[hs][:, s * 512:(s + 1) * 512]
                        t.matmul(pst, mat("BT"), u_s, start=True, stop=False)
                        cT.inc(t.matmul(pst, mat("AT"), hprev,
                                        start=False, stop=True))
                        _tw(t, psAux_free[hs])
                        t.matmul(pax[:, 0:512], mat("CBD"), u_s,
                                 start=True, stop=False)
                        cT.inc(t.matmul(pax[:, 0:512], mat("CA"), hprev,
                                        start=False, stop=True))
                    block.tensor(f)
                    y_tok = cT.n

                    def f2(v):
                        _w(v, cT, y_tok)
                        if s < L - 1:
                            cV.inc(v.tensor_copy(hrot[:, 0:512], pst))
                            st["hev"][100 + s] = cV.n
                        if s == 0:
                            _tw(v, ystag_free[hs])
                        cV.inc(v.tensor_copy(ystag[hs][:, s:4096:8],
                                             pax[:, 0:512]))
                    block.vector(f2)
                    psSt_free[hs] = (cV, cV.n) if s < L - 1 else (cT, y_tok)
                    psAux_free[hs] = (cV, cV.n)
                    st["yev"] = cV.n
                return th
            for s in range(L):
                thunks.append(t_p2(s))

            def t_yst():
                def f(sc):
                    _w(sc, cV, st["yev"])
                    cA.inc(sc.copy(ystag[hs][:], ystag[hs][:]))
                    sc.wait_ge(cA.h, cA.n)
                    sc.dma_start(y16[h * 128:(h + 1) * 128, :], ystag[hs][:]
                                 ).then_inc(dS.h, 16)
                    dS.n += 16
                block.scalar(f)
                ystag_free[hs] = (dS, dS.n)
                upk_free[hs] = (cT, cT.n)
            thunks.append(t_yst)
            return thunks

        if "ssm" in phases:
            for g0 in range(0, NH, NP):
                groups = [ssm_head_thunks(hh) for hh in range(g0, min(g0 + NP, NH))]
                for row in zip(*groups):
                    for th in row:
                        th()
            ssm_end_cT = cT.n
            ssm_y_done = dS.n
            psB_free[0] = psAux_free[1]
            psB_free[1] = psAux_free[2]


        yseg = arena1[:, 0:8192]
        t1tok = arena2[:, 8192:16384]
        f1b = arena1[:, 0:16384]
        gb = arena1[:, 0:32768]
        afeat = arena2[:, 0:8192]
        f2b = arena2[:, 0:16384]
        xnfs2 = arena3[:, 0:8192]
        m1feat = arena3[:, 8192:16384]
        atok = arena4[:]
        outacc = facc

        prev = {"merge_mm": ssm_end_cT, "m1_mm": ssm_end_cT,
                "acc_dma": 0}
        oacc_free = None
        wsl_free = [None, None]
        lps = [psA[:, i * 512:(i + 1) * 512] for i in range(4)] + \
              [psB[:, i * 512:(i + 1) * 512] for i in range(2)]
        if "ssm" in phases:
            lp_free = [psSt_free[0], psSt_free[1], psSt_free[2],
                       psAux_free[0], psAux_free[1], psAux_free[2]]
        else:
            lp_free = [None] * 6

        def stream_fo_stage(wsrc16, kdim, fodim, rhs_buf, rhs_gate, dst,
                            bias_key, act, pre_name, seg, mm_gate=None):
            nonlocal wsl_free
            src3 = wsrc16.rearrange("(kb p) m -> p kb m", p=128)
            grp = 2 if kdim == 16 else 1
            cols = 128 * grp
            ev_tok = 0
            for fg in range(fodim // grp):
                wslot = fg % 2

                def sdma(sy):
                    _tw(sy, wsl_free[wslot])
                    if seg == 0 and fg == 0:
                        _w(sy, dG, pre_tok[pre_name])
                    sy.dma_start(
                        wslab[:, wslot * 4096:wslot * 4096 + kdim * cols]
                        .rearrange("p (kb c) -> p kb c", kb=kdim),
                        src3[:, :, fg * cols:(fg + 1) * cols]
                    ).then_inc(dW.h, 16)
                    dW.n += 16

                block.sync(sdma)
                w_tok2 = dW.n
                mm_toks = []
                for sub in range(grp):
                    fo = grp * fg + sub
                    slot = fo % 6
                    ps = lps[slot]

                    def smm(t):
                        if fg == 0 and sub == 0:
                            _tw(t, mm_gate)
                            _tw(t, rhs_gate)
                        _w(t, dW, w_tok2)
                        _tw(t, lp_free[slot])
                        ins = None
                        for k in range(kdim):
                            ins = t.matmul(
                                ps,
                                wslab[:, wslot * 4096 + k * cols + sub * 128:
                                      wslot * 4096 + k * cols + sub * 128 + 128],
                                rhs_buf[:, k * 512:(k + 1) * 512],
                                start=(k == 0), stop=(k == kdim - 1))
                        cT.inc(ins)

                    block.tensor(smm)
                    mm_toks.append(cT.n)

                    if act == AF.Relu:
                        def sev(v):
                            _w(v, cT, mm_toks[-1])
                            cV.inc(v.tensor_scalar(
                                dst[:, fo * 512:(fo + 1) * 512], ps,
                                vecs[bias_key][:, fo:fo + 1], 0.0,
                                op0=ALU.add, op1=ALU.max))

                        block.vector(sev)
                        lp_free[slot] = (cV, cV.n)
                        ev_tok = (cV, cV.n)
                    else:
                        def sev(sc):
                            _w(sc, cT, mm_toks[-1])
                            cA.inc(sc.activation(
                                dst[:, fo * 512:(fo + 1) * 512], ps, act,
                                bias=vecs[bias_key][:, fo:fo + 1], scale=1.0))

                        block.scalar(sev)
                        lp_free[slot] = (cA, cA.n)
                        ev_tok = (cA, cA.n)
                wsl_free[wslot] = (cT, mm_toks[-1])
            return cT.n, ev_tok

        if "p2" in phases:
            for seg in range(NSEG):
                y3 = y16.rearrange("(c p) t -> p c t", p=128)
                xn3 = xnf16.rearrange("(c p) t -> p c t", p=128)

                def p3_ld(sy):
                    _w(sy, cT, prev["merge_mm"])
                    if seg == 0:
                        _w(sy, dS, ssm_y_done)
                    sy.dma_start(
                        yseg.rearrange("p (c t) -> p c t", c=KD),
                        y3[:, :, seg * SEG:(seg + 1) * SEG]).then_inc(dW.h, 16)
                    dW.n += 16
                    sy.dma_start(
                        xnfs2.rearrange("p (c t) -> p c t", c=KD),
                        xn3[:, :, seg * SEG:(seg + 1) * SEG]).then_inc(dW.h, 16)
                    dW.n += 16

                block.sync(p3_ld)
                yseg_tok = dW.n - 16
                xnf_tok = dW.n

                src3 = outw16.rearrange("(kb p) m -> p kb m", p=128)
                for fp in range(KD // 2):
                    wslot = fp % 2

                    def owdma(sy):
                        _tw(sy, wsl_free[wslot])
                        if seg == 0 and fp == 0:
                            _w(sy, dG, pre_tok["outw16"])
                        sy.dma_start(
                            wslab[:, wslot * 4096:wslot * 4096 + 4096]
                            .rearrange("p (kb c) -> p kb c", kb=16),
                            src3[:, :, fp * 256:(fp + 1) * 256]
                        ).then_inc(dW.h, 16)
                        dW.n += 16

                    block.sync(owdma)
                    ow_tok = dW.n
                    op_toks = []
                    for sub in range(2):
                        fo = 2 * fp + sub
                        slot = fo % 6
                        ps = lps[slot]

                        def opmm(t):
                            _w(t, dW, ow_tok)
                            if fp == 0 and sub == 0:
                                _w(t, dW, yseg_tok)
                            _tw(t, lp_free[slot])
                            ins = None
                            for k in range(KD):
                                ins = t.matmul(
                                    ps,
                                    wslab[:, wslot * 4096 + k * 256 +
                                          sub * 128:
                                          wslot * 4096 + k * 256 +
                                          sub * 128 + 128],
                                    yseg[:, k * 512:(k + 1) * 512],
                                    start=(k == 0), stop=(k == KD - 1))
                            cT.inc(ins)

                        block.tensor(opmm)
                        op_toks.append(cT.n)

                        def aev(v):
                            _w(v, cT, op_toks[-1])
                            if fp == 0 and sub == 0:
                                _w(v, dW, xnf_tok)
                                _w(v, cT, prev["merge_mm"])
                            cV.inc(v.scalar_tensor_tensor(
                                afeat[:, fo * 512:(fo + 1) * 512], ps,
                                vecs["outb_c"][:, fo:fo + 1],
                                xnfs2[:, fo * 512:(fo + 1) * 512],
                                op0=ALU.add, op1=ALU.add))

                        block.vector(aev)
                        lp_free[slot] = (cV, cV.n)
                    wsl_free[wslot] = (cT, op_toks[-1])
                a_done = cV.n

                atok_v = atok.rearrange("p (tb c j) -> p tb c j", tb=4, j=128)
                gi = 0
                for tb in range(4):
                    for half in range(2):
                        pslot = gi % 2
                        ps = psTs[pslot]

                        def tra(t):
                            _w(t, cV, a_done)
                            _w(t, cV, psT_last[pslot]["cV"])
                            _w(t, cA, psT_last[pslot]["cA"])
                            ins = None
                            for j in range(8):
                                c = half * 8 + j
                                ins = t.transpose(
                                    ps[:, j * 128:(j + 1) * 128],
                                    afeat[:, c * 512 + tb * 128:
                                             c * 512 + tb * 128 + 128],
                                    ident[:])
                            cT.inc(ins)

                        block.tensor(tra)
                        tr_tok = cT.n

                        def eva(v):
                            _w(v, cT, tr_tok)
                            pv = ps.rearrange("p (c j) -> p c j", j=128)
                            cV.inc(v.tensor_copy(
                                atok_v[:, tb, half * 8:half * 8 + 8, :], pv))

                        block.vector(eva)
                        psT_last[pslot]["cV"] = cV.n
                        gi += 1
                atok_done = cV.n
                atok_mm_last = cT.n

                ats = [atok[:, tb * 2048:(tb + 1) * 2048] for tb in range(4)]
                ln2_tok = stats_batch(
                    ats, gates=[(cV, atok_done)] * 4, skip_s2=True)
                t1_toks = []
                for tb in range(4):
                    def p3_t1(v):
                        if tb == 0:
                            _w(v, cV, ln2_tok)
                            _w(v, cT, prev["merge_mm"])
                        cV.inc(v.tensor_scalar(
                            t1tok[:, tb * 2048:(tb + 1) * 2048], ats[tb],
                            stm[:, 8 * tb:8 * tb + 1],
                            stm[:, 8 * tb + 2:8 * tb + 3],
                            op0=ALU.subtract, op1=ALU.mult))

                    block.vector(p3_t1)
                    t1_toks.append(cV.n)

                _f1_mm, f1_done = stream_fo_stage(
                    W1T16, KD, KF, xnfs2, (dW, xnf_tok), f1b, "fb1_c",
                    AF.Relu, "W1T16", seg)

                gi = 0
                for half in range(2):
                    for tb in range(4):
                        pslot = gi % 2
                        ps = psTs[pslot]

                        def trt(t):
                            _w(t, cV, t1_toks[tb])
                            _w(t, cV, psT_last[pslot]["cV"])
                            _w(t, cA, psT_last[pslot]["cA"])
                            ins = None
                            for j in range(8):
                                c = half * 8 + j
                                ins = t.transpose(
                                    ps[:, j * 128:(j + 1) * 128],
                                    t1tok[:, tb * 2048 + c * 128:
                                             tb * 2048 + (c + 1) * 128],
                                    ident[:])
                            cT.inc(ins)

                        block.tensor(trt)
                        tr_tok = cT.n

                        def evm(v):
                            _w(v, cT, tr_tok)
                            ins = None
                            for j in range(8):
                                c = half * 8 + j
                                ins = v.tensor_scalar(
                                    m1feat[:, c * 512 + tb * 128:
                                              c * 512 + tb * 128 + 128],
                                    ps[:, j * 128:(j + 1) * 128],
                                    vecs["ln2w_c"][:, c:c + 1],
                                    vecs["ln2b_c"][:, c:c + 1],
                                    op0=ALU.mult, op1=ALU.add)
                            cV.inc(ins)

                        block.vector(evm)
                        psT_last[pslot]["cV"] = cV.n
                        gi += 1
                m1_mm_last = cT.n
                m1_done = (cV, cV.n)

                f2_mm, f2_done = stream_fo_stage(
                    W2T16, KF, KF, f1b, f1_done, f2b, "fb2_c",
                    AF.Relu, "W2T16", seg)

                g_mm, g_done = stream_fo_stage(
                    w116, KD, KH, m1feat, m1_done, gb, "b1_c",
                    AF.Gelu, "w116", seg, mm_gate=(cT, f2_mm))

                w2s = w216.rearrange("(kb p) m -> p kb m", p=128)
                w3s = W3T16.rearrange("(kb p) m -> p kb m", p=128)
                merge_ev_prev = 0
                for wv in range(4):
                    wave_start = True
                    for src3b, kdim, kbase, gate, buf in (
                            (w2s, KH, 0, g_done, gb),
                            (w3s, KF, 0, f2_done, f2b)):
                        for kg in range(kdim // 8):
                            kslot = kg % 2

                            def mdma(sy):
                                _tw(sy, wsl_free[kslot])
                                if seg == 0 and wv == 0 and kg == 0:
                                    _w(sy, dG, pre_tok[
                                        "w216" if buf is gb else "W3T16"])
                                sy.dma_start(
                                    wslab[:, kslot * 4096:kslot * 4096 + 4096]
                                    .rearrange("p (kb c) -> p kb c", kb=8),
                                    src3b[:, kg * 8:(kg + 1) * 8,
                                          wv * 512:(wv + 1) * 512]
                                ).then_inc(dW.h, 16)
                                dW.n += 16

                            block.sync(mdma)
                            wtok3 = dW.n
                            first_mm_of_wave = wave_start and kg == 0

                            def mmm(t):
                                _w(t, dW, wtok3)
                                if kg == 0:
                                    _tw(t, gate)
                                if first_mm_of_wave and wv == 0:
                                    for sl4 in range(4):
                                        _tw(t, lp_free[sl4])
                                if first_mm_of_wave and wv > 0:
                                    _w(t, cV, merge_ev_prev)
                                ins = None
                                for k8 in range(8):
                                    k = kg * 8 + k8
                                    st = (first_mm_of_wave and k8 == 0)
                                    for tb in range(4):
                                        ins = t.matmul(
                                            psA[:, tb * 512:(tb + 1) * 512],
                                            buf[:, k * 512 + tb * 128:
                                                   k * 512 + tb * 128 + 128],
                                            wslab[:, kslot * 4096 + k8 * 512:
                                                  kslot * 4096 +
                                                  (k8 + 1) * 512],
                                            start=st, stop=False)
                                cT.inc(ins)

                            block.tensor(mmm)
                            wsl_free[kslot] = (cT, cT.n)
                        wave_start = False

                    def mbias(t):
                        _w(t, cV, b23_tok)
                        ins = None
                        for tb in range(4):
                            ins = t.matmul(psA[:, tb * 512:(tb + 1) * 512],
                                           onescol[:],
                                           b23[:, wv * 512:(wv + 1) * 512],
                                           start=False, stop=True)
                            cT.inc(ins)

                    block.tensor(mbias)
                    wave_toks = [cT.n - 3, cT.n - 2, cT.n - 1, cT.n]

                    def mev(v):
                        _tw(v, oacc_free)
                        for tb in range(4):
                            _w(v, cT, wave_toks[tb])
                            cV.inc(v.tensor_tensor(
                                outacc[:, tb * 2048 + wv * 512:
                                          tb * 2048 + (wv + 1) * 512],
                                psA[:, tb * 512:(tb + 1) * 512],
                                atok[:, tb * 2048 + wv * 512:
                                        tb * 2048 + (wv + 1) * 512],
                                op=ALU.add))

                    block.vector(mev)
                    merge_ev_prev = cV.n
                merge_done = cV.n
                merge_mm_done = cT.n
                for sl4 in range(4):
                    lp_free[sl4] = (cV, merge_done)

                def p3_accf(s):
                    _w(s, cV, merge_done)
                    cA.inc(s.copy(outacc[:], outacc[:]))

                block.scalar(p3_accf)
                accf_tok = cA.n

                def p3_acc(g):
                    _w(g, cA, accf_tok)
                    _w(g, dG, pre_tok["outcopy"])
                    g.dma_start(
                        out[seg * SEG:(seg + 1) * SEG, :].rearrange(
                            "(tb p) m -> p tb m", p=128),
                        outacc.rearrange("p (tb m) -> p tb m", tb=4),
                        accum_op=ALU.add,
                        max_dma_last_dim=8192).then_inc(dG.h, 16)
                    dG.n += 16

                block.gpsimd(p3_acc)
                oacc_free = (dG, dG.n)
                prev = {"merge_mm": merge_mm_done, "m1_mm": m1_mm_last,
                        "acc_dma": dG.n}

        fG, fS, fW = dG.n, dS.n, dW.n

        def fin(g):
            _w(g, dG, fG)
            _w(g, dS, fS)
            _w(g, dW, fW)

        block.gpsimd(fin)

    return nc


def _cols(v, k):
    return np.ascontiguousarray(np.asarray(v, np.float32).reshape(k, 128).T)


def make_in_maps(ii):
    A = np.ascontiguousarray(np.asarray(ii["L0_attn_A"], np.float32))
    Bm = np.ascontiguousarray(np.asarray(ii["L0_attn_Bm"], np.float32))
    Cm = np.ascontiguousarray(np.asarray(ii["L0_attn_Cm"], np.float32))
    Dm = np.ascontiguousarray(np.asarray(ii["L0_attn_Dm"], np.float32))
    base = {
        "inw": np.ascontiguousarray(ii["L0_attn_in_w"], dtype=np.float32),
        "outw": np.ascontiguousarray(ii["L0_attn_out_w"], dtype=np.float32),
        "w1": np.ascontiguousarray(ii["L0_attn_w1"], dtype=np.float32),
        "w2": np.ascontiguousarray(ii["L0_attn_w2"], dtype=np.float32),
        "W1T": np.ascontiguousarray(np.asarray(ii["L0_ffn_W1"], np.float32).T),
        "W2T": np.ascontiguousarray(np.asarray(ii["L0_ffn_W2"], np.float32).T),
        "W3T": np.ascontiguousarray(np.asarray(ii["L0_ffn_W3"], np.float32).T),
        "sA": A, "sB": Bm,
        "sAT": np.ascontiguousarray(A.transpose(0, 2, 1)),
        "sBT": np.ascontiguousarray(Bm.transpose(0, 2, 1)),
        "sCT": np.ascontiguousarray(Cm.transpose(0, 2, 1)),
        "sDT": np.ascontiguousarray(Dm.transpose(0, 2, 1)),
        "ln1w_c": _cols(ii["L0_attn_ln1_w"], KD),
        "ln1b_c": _cols(ii["L0_attn_ln1_b"], KD),
        "inb_c": _cols(ii["L0_attn_in_b"], KD),
        "outb_c": _cols(ii["L0_attn_out_b"], KD),
        "b1_c": _cols(ii["L0_attn_b1"], KH),
        "ln2w_c": _cols(ii["L0_attn_ln2_w"], KD),
        "ln2b_c": _cols(ii["L0_attn_ln2_b"], KD),
        "fb1_c": _cols(ii["L0_ffn_b1"], KF),
        "fb2_c": _cols(ii["L0_ffn_b2"], KF),
        "b2_row": np.ascontiguousarray(
            np.asarray(ii["L0_attn_b2"], np.float32).reshape(1, D)),
        "fb3_row": np.ascontiguousarray(
            np.asarray(ii["L0_ffn_b3"], np.float32).reshape(1, D)),
    }
    xfull = np.asarray(ii["L0_attn_x"], np.float32)
    maps = []
    for b in range(8):
        m = dict(base)
        m["x"] = np.ascontiguousarray(xfull[b])
        maps.append(m)
    return maps


def kernel(**inputs):
    in_maps = make_in_maps(inputs)
    nc = build()
    res = run_bass_kernel_spmd(nc, in_maps, core_ids=list(range(8)))
    return np.stack([np.asarray(r["out"], np.float32) for r in res.results])
